# version 04
# baseline (speedup 1.0000x reference)
from contextlib import ExitStack

import numpy as np
import ml_dtypes
import concourse.bass as bass
import concourse.mybir as mybir
from concourse.bass_utils import run_bass_kernel_spmd
from concourse.expressions import RuntimeValue

F32 = mybir.dt.float32
BF16 = mybir.dt.bfloat16
I32 = mybir.dt.int32
ALU = mybir.AluOpType
ACT = mybir.ActivationFunctionType
AX = mybir.AxisListType
BF = ml_dtypes.bfloat16

B = 2
C = 16384
D = 2048
H = 8192
NCORES = 8
TOKPC = 4096
NT = 8
TT = 512
KC = 16
MC = 64
QMAX = 15.0
CRND = 12582912.0
EPS = 1e-5
NSHR = 16
BS = 64
NROW = 8
YROWS = (NROW + 1) * 2 * BS
XSW = 2064

_S64 = np.linspace(0.4, 1.0, 16)
_S32 = np.float32(_S64)
_C32 = np.float32(1.0) / _S32
_SQ32 = _S32 * _S32
INV15 = float(np.float32(1.0) / np.float32(15.0))

_DEBUG = False


def _plan_mixer(rows_np):
    rows_np = np.asarray(rows_np, np.int64)
    vals = {}
    for g, r in enumerate(rows_np):
        vals.setdefault(int(r), []).append(g)
    order = sorted(vals.items(), key=lambda kv: (-len(kv[1]), kv[0]))
    core_rows = [[] for _ in range(NCORES)]
    core_groups = [[] for _ in range(NCORES)]
    load = [0] * NCORES
    for r, gs in order:
        cand = [c for c in range(NCORES) if len(core_rows[c]) < NROW]
        c = min(cand, key=lambda cc: (load[cc], cc))
        core_rows[c].append(r)
        core_groups[c].append((r, gs))
        load[c] += len(gs)
    return core_rows, core_groups


def _token_layout(core_rows):
    active_ch = set()
    for rl in core_rows:
        for r in rl:
            active_ch.update(range(64 * r, 64 * r + 64))
    pool = [(b, ch) for ch in range(C) if ch not in active_ch for b in range(B)]
    idx = [np.full(TOKPC, -1, np.int64) for _ in range(NCORES)]
    pos = 0
    for c in range(NCORES):
        rl = core_rows[c]
        for slot in range(min(NROW, len(rl))):
            for b in range(B):
                for i in range(BS):
                    idx[c][slot * 128 + b * 64 + i] = b * C + 64 * rl[slot] + i
        for t in range(TOKPC):
            if idx[c][t] < 0:
                bb, ch = pool[pos]
                pos += 1
                idx[c][t] = bb * C + ch
    return idx


def _build(S):
    n_secs = 2 * S
    nc = bass.Bass("TRN2", target_bir_lowering=False, debug=False,
                   dynamic_dma_scratch_size=16384, detect_race_conditions=False)

    x_h = nc.dram_tensor("x", [TOKPC, D], F32, kind="ExternalInput").ap()
    xsrc_h = nc.dram_tensor("xsrc", [S, B, BS, D], F32, kind="ExternalInput").ap()
    convw_h = nc.dram_tensor("convw", [S, 128, 8 * BS], F32, kind="ExternalInput").ap()
    slotrow_h = nc.dram_tensor("slotrow", [1, S], I32, kind="ExternalInput").ap()
    wfc_h = nc.dram_tensor("wfc", [1024, D], F32, kind="ExternalInput").ap()
    wpr_h = nc.dram_tensor("wpr", [D, 1024], F32, kind="ExternalInput").ap()
    ident_h = nc.dram_tensor("ident", [128, 128], F32, kind="ExternalInput").ap()
    out_h = nc.dram_tensor("out", [TOKPC, D], F32, kind="ExternalOutput").ap()

    y_hbm = nc.dram_tensor("y_hbm", [YROWS, D], F32, kind="Internal")
    agin = nc.dram_tensor("agin", [16 * 128, D], BF16, kind="Internal")
    agout = nc.dram_tensor("agout", [NCORES * 16 * 128, D], BF16, kind="Internal")

    dbg = {}
    if _DEBUG:
        dbg["y"] = nc.dram_tensor("dbg_y", [YROWS, D], F32, kind="ExternalOutput").ap()
        dbg["deq"] = nc.dram_tensor("dbg_deq", [NCORES * 2048, D], BF16,
                                    kind="ExternalOutput").ap()
        dbg["u"] = nc.dram_tensor("dbg_u", [TOKPC, D], BF16,
                                  kind="ExternalOutput").ap()
        dbg["h2"] = nc.dram_tensor("dbg_h2", [128, MC * TT], BF16,
                                   kind="ExternalOutput").ap()
        dbg["zsb"] = nc.dram_tensor("dbg_zsb", [128, KC * TT], BF16,
                                    kind="ExternalOutput").ap()
        dbg["uT"] = nc.dram_tensor("dbg_uT", [128, KC * TT], BF16,
                                   kind="ExternalOutput").ap()
        dbg["agin"] = nc.dram_tensor("dbg_agin", [16 * 128, D], BF16,
                                     kind="ExternalOutput").ap()

    NQT = 24
    tI = [2048] * 8 + [1024] * 16
    tNB = [i // 64 for i in tI]
    tNG = [i // 512 for i in tI]
    grp_base = np.cumsum([0] + tNG).tolist()
    n_groups = int(grp_base[NQT])
    sec_tile = [(idx * NQT) // n_secs for idx in range(n_secs)]
    secs_at = [[] for _ in range(NQT)]
    for idx, j in enumerate(sec_tile):
        secs_at[j].append(idx)
    NU = n_secs * 4
    tST = [1] * 8 + [2] * 16
    st_base = np.cumsum([0] + tST).tolist()
    n_stores = int(st_base[NQT])

    tr_grp_base = {}
    ztr_grp_base = {}
    gcnt = 0
    tr_grp_base[0] = gcnt; gcnt += KC
    for i in range(NT):
        if i + 1 < NT:
            tr_grp_base[i + 1] = gcnt; gcnt += KC
        ztr_grp_base[i] = gcnt; gcnt += KC

    stack = ExitStack()
    sb = lambda *a, **k: stack.enter_context(nc.sbuf_tensor(*a, **k))
    ps = lambda *a, **k: stack.enter_context(nc.psum_tensor(*a, **k))
    sem = lambda n: stack.enter_context(nc.semaphore(n))

    ident_s = sb("ident_s", [128, 128], F32)
    identb_s = sb("identb_s", [128, 128], BF16)
    slr_s = sb("slr_s", [1, max(S, 2)], I32)
    x_ring = sb("x_ring", [128, 2 * D], F32)
    y_ring = sb("y_ring", [128, D], F32)
    o_ring = sb("o_ring", [128, 2 * D], F32)
    u_tok = [sb("u_tok0", [128, 4 * D], BF16), sb("u_tok1", [128, 4 * D], BF16)]
    scrap = sb("scrap", [128, D], F32)
    st_sum1 = sb("st_sum1", [128, 4], F32)
    st_sum2 = sb("st_sum2", [128, 4], F32)
    st_mean = sb("st_mean", [128, 4], F32)
    st_msq = sb("st_msq", [128, 4], F32)
    st_var = sb("st_var", [128, 4], F32)
    st_rstd = sb("st_rstd", [128, 4], F32)
    scnop_s = sb("scnop_s", [128, 1], F32)
    st_s2d = sb("st_s2d", [128, 4], F32)

    s_misc = sem("s_misc"); s_slr = sem("s_slr")
    s_x = [sem("s_x0"), sem("s_x1")]
    s_y = [sem("s_y0"), sem("s_y1")]
    s_w = [sem("s_w0"), sem("s_w1")]
    s_ag = [sem("s_ag0"), sem("s_ag1")]
    s_xsrc = [sem("s_xsrc0"), sem("s_xsrc1")]
    s_yz = sem("s_yz")
    s_yadd = [sem("s_yadd0"), sem("s_yadd1")]
    s_wfcd = [sem("s_wfcd0"), sem("s_wfcd1"), sem("s_wfcd2")]
    s_wprd = [sem("s_wprd0"), sem("s_wprd1")]
    s_out = [sem("s_out0"), sem("s_out1")]
    agp = [[0] * (NQT + 1), [0] * (NQT + 1)]
    for _j in range(NQT):
        for _p in range(2):
            agp[_p][_j + 1] = agp[_p][_j] + (tST[_j] if _j % 2 == _p else 0)
    c_T = sem("c_T"); c_a = sem("c_a"); c_dd = sem("c_dd"); c_sq = sem("c_sq")
    c_red = sem("c_red"); c_deq = sem("c_deq"); c_tr = sem("c_tr")
    c_dqe = sem("c_dqe"); c_uTe = sem("c_uTe"); c_u = sem("c_u")
    c_yv = sem("c_yv"); c_var1 = sem("c_var1"); c_rstd1 = sem("c_rstd1")
    c_var2 = sem("c_var2"); c_rstd2 = sem("c_rstd2")
    c_s1 = sem("c_s1"); c_m1 = sem("c_m1")
    c_s2 = sem("c_s2"); c_m2 = sem("c_m2")
    c_mm1 = sem("c_mm1"); c_hfree = sem("c_hfree"); c_h2 = sem("c_h2")
    c_wprf = sem("c_wprf"); c_zev = sem("c_zev"); c_o = sem("c_o")
    c_cmmu = sem("c_cmmu"); c_cev = sem("c_cev"); c_z0 = sem("c_z0")
    cc = sem("cc")

    p0 = ExitStack()
    sb0 = lambda *a, **k: p0.enter_context(nc.sbuf_tensor(*a, **k))
    ps0 = lambda *a, **k: p0.enter_context(nc.psum_tensor(*a, **k))
    wt = [sb0("wt0", [128, D], F32), sb0("wt1", [128, D], F32)]
    T_t = sb0("T_t", [128, D], F32)
    a_t = [sb0("a_t0", [128, D], F32), sb0("a_t1", [128, D], F32)]
    r_t = [sb0("r_t0", [128, D], F32), sb0("r_t1", [128, D], F32)]
    d2_t = [sb0("d2_t0", [128, D], F32), sb0("d2_t1", [128, D], F32)]
    dq_t = sb0("dq_t", [128, D], F32)
    amax_t = sb0("amax_t", [128, 32], F32)
    base_t = sb0("base_t", [128, 32], F32)
    invb_t = sb0("invb_t", [128, 32], F32)
    E_t = sb0("E_t", [128, 64], F32)
    E2_t = sb0("E2_t", [128, 64], F32)
    best_e = sb0("best_e", [128, 32], F32)
    best_c = sb0("best_c", [128, 32], F32)
    best_s = sb0("best_s", [128, 32], F32)
    mask_t = sb0("mask_t", [128, 64], F32)
    nmask_t = sb0("nmask_t", [128, 64], F32)
    sig_t = sb0("sig_t", [128, 32], F32)
    isig_t = sb0("isig_t", [128, 32], F32)
    tstage = [sb0("tstage0", [128, D], BF16), sb0("tstage1", [128, D], BF16)]
    xs2b = [sb0("xs2b0", [128, 2 * XSW], BF16), sb0("xs2b1", [128, 2 * XSW], BF16)]
    cwb = [sb0("cwb0", [128, 512], BF16), sb0("cwb1", [128, 512], BF16)]
    cst4 = sb0("cst4", [128, 4 * 512], F32)
    tr_ps = [ps0("tr_ps0", [128, 512], F32), ps0("tr_ps1", [128, 512], F32)]
    cv_ps = [ps0("cv_ps%d" % i, [64, 512], F32) for i in range(4)]
    p0.close()

    uT_s = sb("uT_s", [128, KC * TT], BF16)
    h2_s = sb("h2_s", [128, MC * TT], BF16)
    zsb_s = sb("zsb_s", [128, KC * TT], BF16)
    wfc_r = sb("wfc_r", [128, 3 * D], BF16)
    wpr_r = sb("wpr_r", [128, 16 * 512], BF16)
    hrelu = sb("hrelu", [128, 512], F32)
    trp = [ps("trp0", [128, 512], BF16), ps("trp1", [128, 512], BF16)]
    hp = [ps("hp0", [128, 512], F32), ps("hp1", [128, 512], F32)]
    zf = [ps("zf%d" % i, [128, 512], F32) for i in range(4)]

    def w3(ap2, I):
        return ap2[:, :I].rearrange("p (n b) -> p n b", b=64)

    def bc3(ap2, NB):
        return ap2[:, :NB].unsqueeze(2).broadcast_to([128, NB, 64])

    def wfc_src(m):
        c0, i = m // 8, m % 8
        r0 = 2048 * c0 + 128 * i
        return agout.ap()[r0:r0 + 128, :]

    with nc.Block("main") as blk:

        @blk.sync
        def _(sy):
            sy.dma_start(out=ident_s[:, :], in_=ident_h[:, :]).then_inc(s_misc, 16)
            for sti in range(2):
                sy.dma_start(out=x_ring[:, sti * D:(sti + 1) * D],
                             in_=x_h[sti * 128:(sti + 1) * 128, :]).then_inc(
                    s_x[sti % 2], 16)
            for j in range(2):
                sy.dma_start(out=wt[j % 2][:, :tI[j]],
                             in_=wfc_h[128 * j:128 * (j + 1), :]).then_inc(
                    s_w[j % 2], 16)
            for j in range(NQT):
                for sidx in range(tST[j]):
                    if j < 8:
                        sy.wait_ge(c_dqe, grp_base[j + 1])
                        sy.dma_start(out=agin.ap()[128 * j:128 * (j + 1), :],
                                     in_=tstage[j % 2][:, :]).then_inc(s_ag[j % 2], 16)
                    else:
                        t = j - 8
                        sy.wait_ge(c_dqe, grp_base[j] + sidx + 1)
                        dst = agin.ap()[8 * 128 + sidx * 512:8 * 128 + (sidx + 1) * 512,
                                        128 * t:128 * (t + 1)]
                        dst = dst.rearrange("(i p) c -> p i c", p=128)
                        src = tstage[j % 2][:, 512 * sidx:512 * (sidx + 1)]
                        src = src.rearrange("p (i c) -> p i c", c=128)
                        sy.dma_start(out=dst, in_=src).then_inc(s_ag[j % 2], 16)
                if j + 2 < NQT:
                    jj = j + 2
                    sy.wait_ge(c_deq, j + 1)
                    if jj < 8:
                        src = wfc_h[128 * jj:128 * (jj + 1), :]
                    else:
                        t = jj - 8
                        src = wpr_h[128 * t:128 * (t + 1), 0:1024]
                    sy.dma_start(out=wt[jj % 2][:, :tI[jj]], in_=src).then_inc(
                        s_w[jj % 2], 16)
            for sti in range(2, 4):
                sy.wait_ge(c_u, sti - 1)
                sy.dma_start(out=x_ring[:, (sti % 2) * D:(sti % 2 + 1) * D],
                             in_=x_h[sti * 128:(sti + 1) * 128, :]).then_inc(
                    s_x[sti % 2], 16)
            sy.wait_ge(cc, 1)
            for i in range(NT):
                for m in range(MC):
                    gm = MC * i + m
                    if gm >= 3:
                        sy.wait_ge(c_mm1, gm - 2)
                    sy.dma_start(out=wfc_r[:, (gm % 3) * D:(gm % 3 + 1) * D],
                                 in_=wfc_src(m)).then_inc(s_wfcd[gm % 3], 16)
                    if i + 1 < NT and m in (4, 8, 12, 16):
                        sti = (4, 8, 12, 16).index(m)
                        gsub = 4 * (i + 1) + sti
                        sy.wait_ge(c_u, gsub - 1)
                        sy.dma_start(
                            out=x_ring[:, (gsub % 2) * D:(gsub % 2 + 1) * D],
                            in_=x_h[gsub * 128:(gsub + 1) * 128, :]).then_inc(
                            s_x[gsub % 2], 16)

        @blk.gpsimd
        def _(g):
            g.dma_start(out=slr_s[:, :S], in_=slotrow_h[:, :]).then_inc(s_slr, 16)
            g.wait_ge(c_z0, 1)
            for r in range(YROWS // 128):
                g.dma_start(out=y_hbm.ap()[128 * r:128 * (r + 1), :],
                            in_=o_ring[:, 0:D]).then_inc(s_yz, 16)
            g.wait_ge(c_z0, 2)
            g.wait_ge(s_slr, 16)

            def conv_load(s):
                buf = xs2b[s % 2]
                for b in range(B):
                    bo = b * XSW
                    g.dma_start(out=buf[0:64, bo + 15:bo + 15 + D],
                                in_=xsrc_h[s, b, :, :]).then_inc(s_xsrc[s % 2], 16)
                    g.dma_start(out=buf[64:128, bo + 14:bo + 14 + D],
                                in_=xsrc_h[s, b, :, :]).then_inc(s_xsrc[s % 2], 16)
                g.dma_start(out=cwb[s % 2][:, :], in_=convw_h[s, :, :]).then_inc(
                    s_xsrc[s % 2], 16)

            conv_load(0)
            if S > 1:
                conv_load(1)
            g.wait_ge(s_yz, 16 * (YROWS // 128))
            for s in range(S):
                g.wait_ge(c_cev, 8 * (s + 1))
                sreg = g.alloc_register("soff%d" % s)
                g.reg_load(sreg, slr_s[0:1, s:s + 1])
                rv = RuntimeValue(sreg, min_val=0, max_val=YROWS - 128)
                g.dma_start(out=y_hbm.ap()[bass.ds(rv, 128), :],
                            in_=cst4[:, :],
                            accum_op=ALU.add).then_inc(s_yadd[s % 2], 16)
                g.free_register(sreg)
                if s + 2 < S:
                    g.wait_ge(c_cmmu, 8 * (s + 1))
                    conv_load(s + 2)
            g.wait_ge(s_ag[0], 16 * agp[0][NQT])
            g.wait_ge(s_ag[1], 16 * agp[1][NQT])
            if _DEBUG:
                g.dma_start(out=dbg["agin"], in_=agin.ap()).then_inc(s_yz, 16)
                g.wait_ge(s_yz, 16 * (YROWS // 128) + 16)
            g.collective_compute(
                "AllGather", ALU.bypass,
                replica_groups=[list(range(NCORES))],
                ins=[agin.ap().opt()],
                outs=[agout.ap().opt()],
            ).then_inc(cc, 1)
            g.wait_ge(cc, 1)
            g.wait_ge(s_yadd[0], 16 * ((S + 1) // 2))
            g.wait_ge(s_yadd[1], 16 * (S // 2))
            for sti in range(4):
                if sti >= 1:
                    g.wait_ge(c_yv, sti)
                g.dma_start(out=y_ring[:, :],
                            in_=y_hbm.ap()[sti * 128:(sti + 1) * 128, :]).then_inc(
                    s_y[0], 16)
            if _DEBUG:
                g.dma_start(out=dbg["y"], in_=y_hbm.ap()).then_inc(s_yz, 16)
                g.dma_start(out=dbg["deq"], in_=agout.ap()).then_inc(s_yz, 16)
            for i in range(NT):
                if _DEBUG:
                    g.wait_ge(c_u, 4 * (i + 1))
                    g.dma_start(out=dbg["u"][512 * i:512 * (i + 1), :].rearrange(
                                    "(s p) d -> p s d", p=128),
                                in_=u_tok[i % 2][:, :].rearrange(
                                    "p (s d) -> p s d", d=D)).then_inc(s_yz, 16)
                    if i == 0:
                        g.wait_ge(c_uTe, KC)
                        g.dma_start(out=dbg["uT"], in_=uT_s[:, :]).then_inc(s_yz, 16)
                    if i == 1:
                        g.wait_ge(c_h2, MC)
                        g.dma_start(out=dbg["h2"], in_=h2_s[:, :]).then_inc(s_yz, 16)
                        g.wait_ge(c_zev, 16)
                        g.dma_start(out=dbg["zsb"], in_=zsb_s[:, :]).then_inc(s_yz, 16)
                if i == 0:
                    for sti in range(4):
                        gsub = 4 + sti
                        g.wait_ge(c_yv, gsub)
                        g.dma_start(
                            out=y_ring[:, :],
                            in_=y_hbm.ap()[gsub * 128:(gsub + 1) * 128, :]).then_inc(
                            s_y[0], 16)
                if i > 0:
                    for sti in range(4):
                        go = 4 * (i - 1) + sti
                        g.wait_ge(c_o, go + 1)
                        g.dma_start(out=out_h[go * 128:(go + 1) * 128, :],
                                    in_=o_ring[:, (go % 2) * D:(go % 2 + 1) * D]
                                    ).then_inc(s_out[go % 2], 16)
                for dgrp in range(4):
                    for c0 in range(8):
                        gg = 32 * i + 8 * dgrp + c0
                        if gg >= 2:
                            g.wait_ge(c_wprf, gg - 1)
                        src = agout.ap()[2048 * c0 + 1024:2048 * c0 + 2048,
                                         512 * dgrp:512 * (dgrp + 1)]
                        src = src.rearrange("(i p) c -> p i c", p=128)
                        grp_slot = (gg % 2) * 8
                        dstv = wpr_r[:, grp_slot * 512:(grp_slot + 8) * 512]
                        dstv = dstv.rearrange("p (i c) -> p i c", c=512)
                        g.dma_start(out=dstv, in_=src).then_inc(s_wprd[gg % 2], 16)
            for sti in range(4):
                go = 4 * (NT - 1) + sti
                g.wait_ge(c_o, go + 1)
                g.dma_start(out=out_h[go * 128:(go + 1) * 128, :],
                            in_=o_ring[:, (go % 2) * D:(go % 2 + 1) * D]
                            ).then_inc(s_out[go % 2], 16)
            if _DEBUG:
                g.wait_ge(s_yz, 16 * (YROWS // 128) + 16 * (3 + NT + 3))
            g.wait_ge(s_out[0], 16 * 2 * NT)
            g.wait_ge(s_out[1], 16 * 2 * NT)

        @blk.vector
        def _(v):
            def vn(n):
                for _ in range(n):
                    v.engine_nop()
            v.memset(o_ring[:, 0:D], 0.0).then_inc(c_z0, 1)
            v.wait_ge(s_misc, 16)
            v.tensor_copy(identb_s[:, :], ident_s[:, :])
            for bi in range(2):
                v.memset(xs2b[bi][:, 0:15], 0.0)
            v.memset(xs2b[0][:, XSW:XSW + 15], 0.0)
            v.memset(xs2b[1][:, XSW:XSW + 15], 0.0).then_inc(c_z0, 1)
            for j in range(NQT):
                I, NB = tI[j], tNB[j]
                w = wt[j % 2]
                v.wait_ge(s_w[j % 2], 16 * (j // 2 + 1))
                v.reduce_max(amax_t[:, :NB], w3(w, I), axis=AX.X,
                             apply_absolute_value=True)
                vn(3)
                v.tensor_scalar(base_t[:, :NB], amax_t[:, :NB], 1e-8, None, ALU.max)
                vn(3)
                v.tensor_scalar(base_t[:, :NB], base_t[:, :NB], INV15, None, ALU.mult)
                vn(3)
                v.reciprocal(invb_t[:, :NB], base_t[:, :NB])
                vn(4)
                v.tensor_tensor(w3(T_t, I), w3(w, I), bc3(invb_t, NB),
                                ALU.mult).then_inc(c_T, 1)
                def sl(buf, par):
                    return buf[:, 32 * par:32 * par + NB]
                for q in range(NSHR + 4):
                    gq = NSHR * j + q
                    qq = q - 1
                    p3 = q - 3
                    p4 = q - 4
                    pm = q - 2
                    if q < NSHR:
                        p = q % 2
                        v.wait_ge(c_a, gq + 1)
                        if gq >= 2:
                            v.wait_ge(c_sq, gq - 1)
                        v.tensor_scalar(r_t[p][:, :I], a_t[p][:, :I],
                                        CRND, CRND, ALU.add, ALU.subtract)
                    else:
                        vn(3)
                    if 1 <= p3 <= NSHR - 1:
                        v.tensor_tensor(best_c[:, :NB], best_c[:, :NB],
                                        sl(nmask_t, p3 % 2), ALU.mult)
                    if 1 <= p4 <= NSHR - 1:
                        v.tensor_tensor(best_s[:, :NB], best_s[:, :NB],
                                        sl(nmask_t, p4 % 2), ALU.mult)
                    if 1 <= q <= NSHR:
                        v.wait_ge(c_sq, NSHR * j + qq + 1)
                        v.reduce_sum(sl(E_t, qq % 2), w3(d2_t[qq % 2], I),
                                     axis=AX.X).then_inc(c_red, 1)
                    else:
                        vn(3)
                    if 1 <= p3 <= NSHR - 1:
                        v.scalar_tensor_tensor(best_c[:, :NB], sl(mask_t, p3 % 2),
                                               float(_C32[p3]), best_c[:, :NB],
                                               ALU.mult, ALU.add)
                    if 1 <= p4 <= NSHR - 1:
                        v.scalar_tensor_tensor(best_s[:, :NB], sl(mask_t, p4 % 2),
                                               float(_S32[p4]), best_s[:, :NB],
                                               ALU.mult, ALU.add)
                    if q < NSHR:
                        p = q % 2
                        v.scalar_tensor_tensor(r_t[p][:, :I], r_t[p][:, :I],
                                               QMAX, a_t[p][:, :I],
                                               ALU.min, ALU.subtract).then_inc(c_dd, 1)
                    else:
                        vn(3)
                    if pm == 0:
                        v.tensor_scalar(best_e[:, :NB], sl(E2_t, 0), 1.0, None,
                                        ALU.mult)
                        v.memset(best_c[:, :NB], float(_C32[0]))
                        v.memset(best_s[:, :NB], float(_S32[0]))
                    elif 1 <= pm <= NSHR - 1:
                        v.tensor_tensor(sl(mask_t, pm % 2), sl(E2_t, pm % 2),
                                        best_e[:, :NB], ALU.is_lt)
                        v.tensor_tensor(sl(nmask_t, pm % 2), sl(E2_t, pm % 2),
                                        best_e[:, :NB], ALU.is_ge)
                        v.tensor_tensor(best_e[:, :NB], best_e[:, :NB],
                                        sl(E2_t, pm % 2), ALU.min)
                    if 1 <= q <= NSHR:
                        v.tensor_scalar(sl(E2_t, qq % 2), sl(E_t, qq % 2),
                                        float(_SQ32[qq]), None, ALU.mult)
                    if j > 0 and q in (3, 6, 9, 12):
                        gloc = (3, 6, 9, 12).index(q)
                        if gloc < tNG[j - 1]:
                            gidx = grp_base[j - 1] + gloc
                            v.wait_ge(c_tr, gidx + 1)
                            if j >= 2 and gloc == 0:
                                v.wait_ge(s_ag[(j - 1) % 2], 16 * agp[(j - 1) % 2][j - 1])
                            v.tensor_copy(
                                tstage[(j - 1) % 2][:, 512 * gloc:512 * (gloc + 1)],
                                tr_ps[gidx % 2][:, :]).then_inc(c_dqe, 1)
                if j > 0:
                    v.wait_ge(c_tr, grp_base[j])
                vn(4)
                v.tensor_tensor(sig_t[:, :NB], base_t[:, :NB], best_s[:, :NB],
                                ALU.mult)
                v.tensor_tensor(isig_t[:, :NB], invb_t[:, :NB], best_c[:, :NB],
                                ALU.mult)
                vn(4)
                v.tensor_tensor(w3(T_t, I), w3(w, I), bc3(isig_t, NB), ALU.mult)
                vn(3)
                v.tensor_scalar(r_t[0][:, :I], T_t[:, :I], CRND, CRND,
                                ALU.add, ALU.subtract)
                vn(3)
                v.tensor_scalar(r_t[0][:, :I], r_t[0][:, :I], QMAX, -QMAX,
                                ALU.min, ALU.max)
                vn(3)
                v.tensor_tensor(w3(dq_t, I), w3(r_t[0], I), bc3(sig_t, NB),
                                ALU.mult).then_inc(c_deq, 1)
            for gloc in range(tNG[NQT - 1]):
                gidx = grp_base[NQT - 1] + gloc
                v.wait_ge(c_tr, gidx + 1)
                if gloc == 0:
                    v.wait_ge(s_ag[(NQT - 1) % 2], 16 * agp[(NQT - 1) % 2][NQT - 1])
                v.tensor_copy(tstage[(NQT - 1) % 2][:, 512 * gloc:512 * (gloc + 1)],
                              tr_ps[gidx % 2][:, :]).then_inc(c_dqe, 1)

            def ln1(ti):
                def apply1(g):
                    sl = g % 2
                    v.tensor_scalar(u_tok[ti % 2][:, (g - 4 * ti) * D:
                                                  (g - 4 * ti + 1) * D],
                                    x_ring[:, sl * D:(sl + 1) * D],
                                    st_mean[:, sl:sl + 1], st_rstd[:, sl:sl + 1],
                                    ALU.subtract, ALU.mult).then_inc(c_u, 1)
                for sti in range(4):
                    gsub = 4 * ti + sti
                    slot = gsub % 2
                    xv = x_ring[:, slot * D:(slot + 1) * D]
                    v.wait_ge(s_x[gsub % 2], 16 * (gsub // 2 + 1))
                    if ti < 2:
                        v.wait_ge(s_y[0], 16 * (gsub + 1))
                        v.tensor_tensor(xv, xv, y_ring[:, :],
                                        ALU.add).then_inc(c_yv, 1)
                        v.engine_nop()
                    v.reduce_sum(st_sum1[:, slot:slot + 1], xv, axis=AX.X)
                    v.scalar_tensor_tensor(scrap[:, :], xv, 1.0, xv,
                                           ALU.mult, ALU.mult,
                                           accum_out=st_sum2[:, slot:slot + 1]
                                           ).then_inc(c_s1, 1)
                    v.wait_ge(c_m1, gsub + 1)
                    v.tensor_scalar(st_var[:, slot:slot + 1],
                                    st_s2d[:, slot:slot + 1],
                                    st_msq[:, slot:slot + 1], EPS,
                                    ALU.subtract, ALU.add).then_inc(c_var1, 1)
                    v.wait_ge(c_rstd1, gsub + 1)
                    v.reciprocal(st_rstd[:, slot:slot + 1], st_rstd[:, slot:slot + 1])
                    if sti >= 1:
                        apply1(gsub - 1)
                v.engine_nop(); v.engine_nop(); v.engine_nop()
                apply1(4 * ti + 3)

            def evac_uT(ti):
                for k in range(KC):
                    gk = tr_grp_base[ti] + k
                    v.wait_ge(c_tr, n_groups + gk + 1)
                    v.tensor_copy(uT_s[:, 512 * k:512 * (k + 1)],
                                  trp[gk % 2][:, :]).then_inc(c_uTe, 1)

            def ln2(ti):
                def apply2(g):
                    sl = 2 + (g % 2)
                    ovp = o_ring[:, (g % 2) * D:(g % 2 + 1) * D]
                    v.tensor_scalar(ovp, ovp, st_mean[:, sl:sl + 1],
                                    st_rstd[:, sl:sl + 1],
                                    ALU.subtract, ALU.mult).then_inc(c_o, 1)
                for sti in range(4):
                    gsub = 4 * ti + sti
                    slot = 2 + (gsub % 2)
                    ov = o_ring[:, (gsub % 2) * D:(gsub % 2 + 1) * D]
                    if gsub >= 2:
                        v.wait_ge(s_out[gsub % 2], 16 * (gsub // 2))
                    for seg in range(4):
                        gk = ztr_grp_base[ti] + 4 * sti + seg
                        v.wait_ge(c_tr, n_groups + gk + 1)
                        v.tensor_tensor(ov[:, 512 * seg:512 * (seg + 1)],
                                        trp[gk % 2][:, :],
                                        u_tok[ti % 2][:, sti * D + 512 * seg:
                                                      sti * D + 512 * (seg + 1)],
                                        ALU.add).then_inc(c_uTe, 1)
                    v.engine_nop()
                    v.reduce_sum(st_sum1[:, slot:slot + 1], ov, axis=AX.X)
                    v.scalar_tensor_tensor(scrap[:, :], ov, 1.0,
                                           ov, ALU.mult, ALU.mult,
                                           accum_out=st_sum2[:, slot:slot + 1]
                                           ).then_inc(c_s2, 1)
                    v.wait_ge(c_m2, gsub + 1)
                    v.tensor_scalar(st_var[:, slot:slot + 1],
                                    st_s2d[:, slot:slot + 1],
                                    st_msq[:, slot:slot + 1], EPS,
                                    ALU.subtract, ALU.add).then_inc(c_var2, 1)
                    v.wait_ge(c_rstd2, gsub + 1)
                    v.reciprocal(st_rstd[:, slot:slot + 1], st_rstd[:, slot:slot + 1])
                    if sti >= 1:
                        apply2(gsub - 1)
                v.engine_nop(); v.engine_nop(); v.engine_nop()
                apply2(4 * ti + 3)

            ln1(0)
            evac_uT(0)
            for i in range(NT):
                if i + 1 < NT:
                    ln1(i + 1)
                    evac_uT(i + 1)
                ln2(i)

        @blk.scalar
        def _(sc):
            n_cev = 0
            for j in range(NQT):
                I, NB = tI[j], tNB[j]
                sc.wait_ge(c_T, j + 1)
                for q in range(NSHR):
                    gq = NSHR * j + q
                    p = q % 2
                    if gq >= 2:
                        sc.wait_ge(c_dd, gq - 1)
                    sc.activation(a_t[p][:, :I], T_t[:, :I], ACT.Abs,
                                  bias=0.0, scale=float(_C32[q])).then_inc(c_a, 1)
                    if gq >= 2:
                        sc.wait_ge(c_red, gq - 1)
                    sc.wait_ge(c_dd, gq + 1)
                    sc.activation(d2_t[p][:, :I], r_t[p][:, :I], ACT.Square,
                                  bias=0.0, scale=1.0).then_inc(c_sq, 1)
                for sec in secs_at[j]:
                    s_slot, b = sec // 2, sec % 2
                    for tt in range(4):
                        u = sec * 4 + tt
                        sc.wait_ge(c_cmmu, u + 1)
                        if b == 0 and tt == 0 and s_slot >= 1:
                            sc.wait_ge(s_yadd[(s_slot - 1) % 2],
                                       16 * ((s_slot - 1) // 2 + 1))
                        sc.copy(cst4[64 * b:64 * (b + 1), 512 * tt:512 * (tt + 1)],
                                cv_ps[u % 4][0:64, :]).then_inc(c_cev, 1)
                        n_cev += 1
            assert n_cev == NU

            def rstd1(gsub):
                slot = gsub % 2
                sc.wait_ge(c_s1, gsub + 1)
                sc.activation(st_mean[:, slot:slot + 1], st_sum1[:, slot:slot + 1],
                              ACT.Copy, bias=0.0, scale=1.0 / D)
                sc.activation(st_msq[:, slot:slot + 1], st_sum1[:, slot:slot + 1],
                              ACT.Square, bias=0.0, scale=1.0 / D)
                sc.activation(st_s2d[:, slot:slot + 1], st_sum2[:, slot:slot + 1],
                              ACT.Copy, bias=0.0, scale=1.0 / D).then_inc(c_m1, 1)
                sc.wait_ge(c_var1, gsub + 1)
                sc.activation(st_rstd[:, slot:slot + 1], st_var[:, slot:slot + 1],
                              ACT.Sqrt, bias=0.0, scale=1.0).then_inc(c_rstd1, 1)

            def rstd2(gsub):
                slot = 2 + (gsub % 2)
                sc.wait_ge(c_s2, gsub + 1)
                sc.activation(st_mean[:, slot:slot + 1], st_sum1[:, slot:slot + 1],
                              ACT.Copy, bias=0.0, scale=1.0 / D)
                sc.activation(st_msq[:, slot:slot + 1], st_sum1[:, slot:slot + 1],
                              ACT.Square, bias=0.0, scale=1.0 / D)
                sc.activation(st_s2d[:, slot:slot + 1], st_sum2[:, slot:slot + 1],
                              ACT.Copy, bias=0.0, scale=1.0 / D).then_inc(c_m2, 1)
                sc.wait_ge(c_var2, gsub + 1)
                sc.activation(st_rstd[:, slot:slot + 1], st_var[:, slot:slot + 1],
                              ACT.Sqrt, bias=0.0, scale=1.0).then_inc(c_rstd2, 1)

            for gsub in range(4):
                rstd1(gsub)
            for i in range(NT):
                for m in range(MC):
                    gm = MC * i + m
                    sc.wait_ge(c_mm1, gm + 1)
                    sc.activation(hrelu[:, :],
                                  hp[m % 2][:, :], ACT.Relu, bias=0.0,
                                  scale=1.0).then_inc(c_hfree, 1)
                    sc.memzero(scnop_s[:, :])
                    sc.activation(h2_s[:, 512 * m:512 * (m + 1)],
                                  hrelu[:, :],
                                  ACT.Square, bias=0.0, scale=1.0).then_inc(c_h2, 1)
                    if i + 1 < NT and m in (5, 9, 13, 17):
                        rstd1(4 * (i + 1) + (5, 9, 13, 17).index(m))
                for dgrp in range(4):
                    sc.wait_ge(c_wprf, 8 * (4 * i + dgrp + 1))
                    for dc in range(4):
                        ch = 4 * dgrp + dc
                        sc.activation(zsb_s[:, 512 * ch:512 * (ch + 1)],
                                      zf[dc][:, :], ACT.Copy, bias=0.0,
                                      scale=1.0).then_inc(c_zev, 1)
                for sti in range(4):
                    rstd2(4 * i + sti)

        @blk.tensor
        def _(te):
            te.wait_ge(s_misc, 16)
            n_cu = 0
            for j in range(NQT):
                for sec in secs_at[j]:
                    s, b = sec // 2, sec % 2
                    te.wait_ge(s_xsrc[s % 2], 16 * 5 * (s // 2 + 1))
                    bo = b * XSW
                    for tt in range(4):
                        u = n_cu
                        if u >= 4:
                            te.wait_ge(c_cev, u - 3)
                        last = None
                        for tp in range(8):
                            last = te.matmul(
                                cv_ps[u % 4][0:64, :],
                                cwb[s % 2][:, 64 * tp:64 * (tp + 1)],
                                xs2b[s % 2][:, bo + 512 * tt + 2 * tp:
                                            bo + 512 * tt + 2 * tp + 512],
                                start=(tp == 0), stop=(tp == 7))
                        last.then_inc(c_cmmu, 1)
                        n_cu += 1
                te.wait_ge(c_deq, j + 1)
                for gloc in range(tNG[j]):
                    gidx = grp_base[j] + gloc
                    if gidx >= 2:
                        te.wait_ge(c_dqe, gidx - 1)
                    last = None
                    for s4 in range(4):
                        k = 4 * gloc + s4
                        last = te.transpose(tr_ps[gidx % 2][:, 128 * s4:128 * (s4 + 1)],
                                            dq_t[:, 128 * k:128 * (k + 1)],
                                            ident_s[:, :])
                    last.then_inc(c_tr, 1)
            assert n_cu == NU

            te.wait_ge(cc, 1)
            evac_done = {0: KC}
            cnt = KC
            for i in range(NT):
                if i + 1 < NT:
                    cnt += KC
                    evac_done[i + 1] = cnt
                cnt += KC

            def tr(ti):
                te.wait_ge(c_u, 4 * (ti + 1))
                for k in range(KC):
                    gk = tr_grp_base[ti] + k
                    if gk >= 2:
                        te.wait_ge(c_uTe, gk - 1)
                    last = None
                    for s4 in range(4):
                        cb = s4 * D + 128 * k
                        last = te.transpose(trp[gk % 2][:, 128 * s4:128 * (s4 + 1)],
                                            u_tok[ti % 2][:, cb:cb + 128],
                                            identb_s[:, :])
                    last.then_inc(c_tr, 1)

            def ztr(ti):
                te.wait_ge(c_zev, 16 * (ti + 1))
                for sti in range(4):
                    for seg in range(4):
                        gk = ztr_grp_base[ti] + 4 * sti + seg
                        if gk >= 2:
                            te.wait_ge(c_uTe, gk - 1)
                        last = None
                        for dc4 in range(4):
                            ch = 4 * seg + dc4
                            last = te.transpose(
                                trp[gk % 2][:, 128 * dc4:128 * (dc4 + 1)],
                                zsb_s[:, 512 * ch + 128 * sti:
                                      512 * ch + 128 * (sti + 1)],
                                identb_s[:, :])
                        last.then_inc(c_tr, 1)

            def mm1(ti):
                te.wait_ge(c_uTe, evac_done[ti])
                for m in range(MC):
                    gm = MC * ti + m
                    te.wait_ge(s_wfcd[gm % 3], 16 * (gm // 3 + 1))
                    if gm >= 2:
                        te.wait_ge(c_hfree, gm - 1)
                    last = None
                    for k in range(KC):
                        last = te.matmul(hp[m % 2][:, :],
                                         wfc_r[:, (gm % 3) * D + 128 * k:
                                               (gm % 3) * D + 128 * (k + 1)],
                                         uT_s[:, 512 * k:512 * (k + 1)],
                                         start=(k == 0), stop=(k == KC - 1))
                    last.then_inc(c_mm1, 1)

            def mm2(ti):
                te.wait_ge(c_h2, MC * (ti + 1))
                for dgrp in range(4):
                    if ti > 0 or dgrp > 0:
                        te.wait_ge(c_zev, 4 * (4 * ti + dgrp))
                    for kg in range(8):
                        gg = 32 * ti + 8 * dgrp + kg
                        te.wait_ge(s_wprd[gg % 2], 16 * (gg // 2 + 1))
                        grp_slot = (gg % 2) * 8
                        last = None
                        for k8 in range(8):
                            k = 8 * kg + k8
                            for dc in range(4):
                                last = te.matmul(
                                    zf[dc][:, :],
                                    wpr_r[:, (grp_slot + k8) * 512 + 128 * dc:
                                          (grp_slot + k8) * 512 + 128 * (dc + 1)],
                                    h2_s[:, 512 * k:512 * (k + 1)],
                                    start=(k == 0), stop=(k == MC - 1))
                        last.then_inc(c_wprf, 1)

            tr(0)
            for i in range(NT):
                mm1(i)
                if i + 1 < NT:
                    tr(i + 1)
                mm2(i)
                ztr(i)

    stack.close()
    return nc



def kernel(L0_attn_x, L0_attn_block_values, L0_attn_cols, L0_attn_rows,
           L0_ffn_w_fc, L0_ffn_w_proj):
    x = np.ascontiguousarray(np.asarray(L0_attn_x, np.float32))
    bv = np.ascontiguousarray(np.asarray(L0_attn_block_values, np.float32))
    cols = np.asarray(L0_attn_cols, np.int64)
    rows = np.asarray(L0_attn_rows, np.int64)
    wfc = np.ascontiguousarray(np.asarray(L0_ffn_w_fc, np.float32))
    wpr = np.asarray(L0_ffn_w_proj, np.float32)

    core_rows, core_groups = _plan_mixer(rows)
    tok_idx = _token_layout(core_rows)

    slots = [[] for _ in range(NCORES)]
    for c in range(NCORES):
        for ri, (r, gs) in enumerate(core_groups[c]):
            for g in gs:
                slots[c].append((g, ri))
    S = max(1, max(len(s) for s in slots))

    xf = x.reshape(B * C, D)
    bvt = bv.transpose(0, 3, 2, 1)
    w2 = np.zeros((128, 8, 128, BS), np.float32)
    w2[:, :, 0:64, :] = bvt[:, 0::2, :, :]
    w2[:, :, 64:128, :] = bvt[:, 1::2, :, :]
    w2 = np.ascontiguousarray(w2.transpose(0, 2, 1, 3).reshape(128, 128, 8 * BS))

    ident = np.eye(128, dtype=np.float32)
    in_maps = []
    for c in range(NCORES):
        xsrc_np = np.zeros((S, B, BS, D), np.float32)
        convw_np = np.zeros((S, 128, 8 * BS), np.float32)
        slotrow_np = np.full((1, S), NROW * 128, np.int32)
        for si, (g, ri) in enumerate(slots[c]):
            cb = int(cols[g])
            xsrc_np[si, :, :, :] = x[:, 64 * cb:64 * cb + 64, :]
            convw_np[si] = w2[g]
            slotrow_np[0, si] = ri * 128
        in_maps.append({
            "x": np.ascontiguousarray(xf[tok_idx[c]]),
            "xsrc": xsrc_np,
            "convw": convw_np,
            "slotrow": slotrow_np,
            "wfc": wfc[1024 * c:1024 * (c + 1), :],
            "wpr": np.ascontiguousarray(wpr[:, 1024 * c:1024 * (c + 1)]),
            "ident": ident,
        })

    nc = _build(S)
    res = run_bass_kernel_spmd(nc, in_maps, core_ids=list(range(NCORES)))

    out = np.empty((B * C, D), np.float32)
    for c in range(NCORES):
        out[tok_idx[c]] = res.results[c]["out"]
    out = out.reshape(B, C, D)
    if _DEBUG:
        dbgl = [{k: res.results[c].get("dbg_" + k)
                 for k in ("y", "deq", "u", "h2", "zsb", "uT", "agin")}
                for c in range(NCORES)]
        return out, dbgl, tok_idx, slots, core_rows
    return out
